# version 03
# baseline (speedup 1.0000x reference)
import contextlib
import numpy as np
import ml_dtypes
import concourse.bass as bass
import concourse.mybir as mybir
from concourse.bass_utils import run_bass_kernel_spmd

F32 = mybir.dt.float32
BF16 = mybir.dt.bfloat16
F8 = mybir.dt.float8e4
AF = mybir.ActivationFunctionType
ALU = mybir.AluOpType
AX = mybir.AxisListType

NT = 8
C = 2048
CT = 16
H = 16
HID = 8192
EPS32 = float(np.finfo(np.float32).eps)
SCALE_O = float(np.float32(128.0 ** -0.5))
N_QKV = 6144 * 2048
N_PROJ = 2048 * 2048


class Ctr:
    def __init__(self, sem):
        self.sem = sem
        self.n = 0

    def inc(self, instr, k=1):
        instr.then_inc(self.sem, k)
        self.n += k
        return self.n

    def wait(self, eng, at=None):
        eng.wait_ge(self.sem, self.n if at is None else at)


class _RecInstr:
    __slots__ = ("incs",)

    def __init__(self):
        self.incs = []

    def then_inc(self, sem, n):
        self.incs.append((sem, n))
        return self


class _Rec:

    def __init__(self, label):
        self._label = label
        self.ops = []

    def wait_ge(self, sem, n):
        self.ops.append(("wait_ge", (sem, n), {}, None))

    def __getattr__(self, name):
        if name.startswith("_"):
            raise AttributeError(name)

        def call(*a, **k):
            m = _RecInstr()
            self.ops.append((name, a, k, m))
            return m
        return call


def build_nc(debug=False):
    nc = bass.Bass("TRN2", target_bir_lowering=False)

    x_own = nc.dram_tensor("x_own", [1024, C], F32, kind="ExternalInput").ap()
    x_prev = nc.dram_tensor("x_prev", [1024, C], F32, kind="ExternalInput").ap()
    qkv_w = nc.dram_tensor("qkv_w", [6144, C], F32, kind="ExternalInput").ap()
    proj_w = nc.dram_tensor("proj_w", [C, C], F32, kind="ExternalInput").ap()
    wg = nc.dram_tensor("wg", [2 * C, HID], F32, kind="ExternalInput").ap()
    wl = nc.dram_tensor("wl", [HID, C], F32, kind="ExternalInput").ap()
    nrm = nc.dram_tensor("nrm", [3, C], F32, kind="ExternalInput").ap()
    svec = nc.dram_tensor("svec", [1, H], F32, kind="ExternalInput").ap()
    ident = nc.dram_tensor("ident", [128, 128], F32, kind="ExternalInput").ap()
    identb = nc.dram_tensor("identb", [128, 128], BF16, kind="ExternalInput").ap()
    onescol = nc.dram_tensor("onescol", [128, 1], F32, kind="ExternalInput").ap()
    onesrow = nc.dram_tensor("onesrow", [1, 128], F32, kind="ExternalInput").ap()
    ttt = nc.dram_tensor("ttt", [128, 128], F32, kind="ExternalInput").ap()
    ttrev = nc.dram_tensor("ttrev", [128, 128], F32, kind="ExternalInput").ap()
    difft = nc.dram_tensor("difft", [128, 128], F32, kind="ExternalInput").ap()

    out = nc.dram_tensor("out", [1024, C], F32, kind="ExternalOutput").ap()
    o_scr = nc.dram_tensor("o_scr", [1024, C], F32, kind="Internal").ap()
    dbg = {}
    if debug:
        for name, shape in [("d_q", [128, 1024]), ("d_k", [128, 1024]), ("d_v", [128, 1024]),
                            ("d_o", [1024, C]), ("d_g", [128, 1024]), ("d_S", [128, 128]),
                            ("d_scal", [1, 8]), ("d_xn8", [128, 1024]), ("d_h", [128, 1024]),
                            ("d_sout", [128, 1024]), ("d_mask", [128, 128]), ("d_lamt", [128, 128]),
                            ("d_sco", [128, 128]), ("d_S0", [128, 128]), ("d_vst", [128, 512]),
                            ("d_oj0", [128, 128]), ("d_negs", [128, 16])]:
            dbg[name] = nc.dram_tensor(name, shape, F32, kind="ExternalOutput").ap()

    es0 = contextlib.ExitStack()
    with es0:
        c_id = es0.enter_context(nc.sbuf_tensor("c_id", [128, 128], F32))
        c_idb = es0.enter_context(nc.sbuf_tensor("c_idb", [128, 128], BF16))
        c_ones = es0.enter_context(nc.sbuf_tensor("c_ones", [128, 1], F32))
        c_onesrow = es0.enter_context(nc.sbuf_tensor("c_onesrow", [1, 128], F32))
        c_ttt = es0.enter_context(nc.sbuf_tensor("c_ttt", [128, 128], F32))
        c_ttrev = es0.enter_context(nc.sbuf_tensor("c_ttrev", [128, 128], F32))
        c_difft = es0.enter_context(nc.sbuf_tensor("c_difft", [128, 128], F32))
        wn = es0.enter_context(nc.sbuf_tensor("wn", [128, C], F32))
        sb = es0.enter_context(nc.sbuf_tensor("sb", [128, H], F32))
        negs = es0.enter_context(nc.sbuf_tensor("negs", [128, H], F32))
        lamT = es0.enter_context(nc.sbuf_tensor("lamT", [128, H], F32))
        scal = es0.enter_context(nc.sbuf_tensor("scal", [128, 8], F32))
        stat = es0.enter_context(nc.sbuf_tensor("stat", [128, 96], F32))
        scr = es0.enter_context(nc.sbuf_tensor("scr", [128, C], F32))
        xn8 = es0.enter_context(nc.sbuf_tensor("xn8", [128, CT * 1024], F8))
        Sst = es0.enter_context(nc.sbuf_tensor("Sst", [128, H * 128], F32))
        pq = es0.enter_context(nc.psum_tensor("pq", [128, 1024], F32))
        pk = es0.enter_context(nc.psum_tensor("pk", [128, 1024], F32))
        pv = es0.enter_context(nc.psum_tensor("pv", [128, 1024], F32))
        pa = es0.enter_context(nc.psum_tensor("pa", [128, 512], F32))
        pb = es0.enter_context(nc.psum_tensor("pb", [128, 512], F32))
        semW = es0.enter_context(nc.semaphore("dW"))
        semX = es0.enter_context(nc.semaphore("dX"))
        semV = es0.enter_context(nc.semaphore("sv"))
        semS = es0.enter_context(nc.semaphore("ss"))
        semT = es0.enter_context(nc.semaphore("st"))
        semG = es0.enter_context(nc.semaphore("sg"))
        block = es0.enter_context(nc.Block())
        dW = Ctr(semW); dX = Ctr(semX)
        sv = Ctr(semV); ss = Ctr(semS); st = Ctr(semT); sg = Ctr(semG)
        ALLC = [dW, dX, sv, ss, st, sg]
        sync, gps, vec, sca, ten = (_Rec("sync"), _Rec("gps"), _Rec("vec"),
                                    _Rec("sca"), _Rec("ten"))
        ENGS = (sync, gps, vec, sca, ten)

        def barrier():
            for eng in ENGS:
                for ctr in ALLC:
                    eng.wait_ge(ctr.sem, ctr.n)

        _uid = [0]

        def sbt(name, shape, dtype):
            _uid[0] += 1
            return nc.sbuf_tensor("%s_u%d" % (name, _uid[0]), shape, dtype)

        def bcast_row(src_row_ap, dst_cols, n=C):
            dX.inc(gps.dma_start(wn[:, dst_cols:dst_cols + n], src_row_ap.broadcast_to([128, n])), 16)

        for dst, src in [(c_id, ident), (c_idb, identb), (c_ones, onescol), (c_onesrow, onesrow),
                         (c_ttt, ttt), (c_ttrev, ttrev), (c_difft, difft)]:
            dX.inc(gps.dma_start(dst[:], src[:]), 16)
        dX.inc(gps.dma_start(sb[:, 0:H], svec[:].broadcast_to([128, H])), 16)
        dX.wait(vec)
        sv.inc(vec.tensor_scalar(negs[:, 0:H], sb[:, 0:H], -1.0, None, ALU.mult))
        sv.wait(sca)
        ss.inc(sca.activation(lamT[:, 0:H], sb[:, 0:H], AF.Exp, scale=-128.0))
        sv.inc(vec.memset(Sst[:], 0.0))

        with sbt("wslab", [128, 2 * C], F32) as wslab:
            vdone = []
            for i in range(48 + 16):
                buf = wslab[:, (i % 2) * C:((i % 2) + 1) * C]
                if i >= 2:
                    sync.wait_ge(semV, vdone[i - 2])
                src = qkv_w[i * 128:(i + 1) * 128, :] if i < 48 else \
                    proj_w[(i - 48) * 128:(i - 47) * 128, :]
                dW.inc(sync.dma_start(buf, src), 16)
                dW.wait(vec)
                sv.inc(vec.tensor_reduce(stat[:, 16 + i:17 + i], buf, AX.X, ALU.add,
                                         apply_absolute_value=True))
                vdone.append(sv.n)
            sv.inc(vec.tensor_reduce(stat[:, 9:10], stat[:, 16:64], AX.X, ALU.add))
            sv.inc(vec.tensor_reduce(stat[:, 10:11], stat[:, 64:80], AX.X, ALU.add))
            sv.wait(ten)
            st.inc(ten.matmul(pa[0:1, 0:1], stat[:, 9:10], c_ones[:], start=True, stop=False))
            st.inc(ten.matmul(pa[0:1, 1:2], stat[:, 10:11], c_ones[:], start=False, stop=True))
            st.wait(vec)
            sv.inc(vec.tensor_scalar(stat[0:1, 0:1], pa[0:1, 0:1], 1.0 / N_QKV, 1e-5, ALU.mult, ALU.max))
            sv.inc(vec.tensor_scalar(stat[0:1, 1:2], pa[0:1, 1:2], 1.0 / N_PROJ, 1e-5, ALU.mult, ALU.max))
            sv.wait(ten)
            st.inc(ten.matmul(pb[:, 0:2], c_onesrow[0:1, :], stat[0:1, 0:2], start=True, stop=True))
            st.wait(vec)
            sv.inc(vec.tensor_scalar(scal[:, 0:1], pb[:, 0:1], 0.5, None, ALU.mult))
            sv.inc(vec.tensor_scalar(scal[:, 1:2], pb[:, 0:1], -0.5, None, ALU.mult))
            sv.inc(vec.tensor_copy(scal[:, 2:3], pb[:, 0:1]))
            sv.inc(vec.tensor_scalar(scal[:, 3:4], pb[:, 1:2], 0.5, None, ALU.mult))
            sv.inc(vec.tensor_scalar(scal[:, 4:5], pb[:, 1:2], -0.5, None, ALU.mult))
            sv.inc(vec.tensor_copy(scal[:, 5:6], pb[:, 1:2]))
            if debug:
                sv.wait(sync)
                dW.inc(sync.dma_start(dbg["d_scal"][:], scal[0:1, 0:8]), 16)
        barrier()

        def emit_ln(x_src, tt, xbuf, want_h):
            st.wait(vec)
            sv.inc(vec.tensor_reduce(stat[:, 0:1], xbuf, AX.X, ALU.add))
            sv.wait(sca)
            ss.inc(sca.activation(stat[:, 1:2], stat[:, 0:1], AF.Copy, scale=-1.0 / 2048.0))
            ss.wait(vec)
            sv.inc(vec.tensor_scalar(scr[:], xbuf, stat[:, 1:2], None, ALU.add))
            sv.wait(sca)
            ss.inc(sca.activation(xbuf, scr[:], AF.Square, accum_out=stat[:, 2:3]))
            ss.wait(vec)
            sv.inc(vec.tensor_scalar(stat[:, 3:4], stat[:, 2:3], 1.0 / 2048.0, 1e-5, ALU.mult, ALU.add))
            sv.wait(sca)
            ss.inc(sca.activation(stat[:, 4:5], stat[:, 3:4], AF.Ln))
            ss.inc(sca.activation(stat[:, 5:6], stat[:, 4:5], AF.Exp, scale=-0.5))
            ss.wait(vec)
            sv.inc(vec.tensor_scalar(scr[:], scr[:], stat[:, 5:6], None, ALU.mult))
            if want_h:
                sv.inc(vec.tensor_tensor(stat[:, 6:7], stat[:, 5:6], stat[:, 5:6], ALU.mult))
                sv.inc(vec.tensor_scalar(stat[:, 7:8], stat[:, 3:4], -1e-5, None, ALU.add))
                sv.inc(vec.tensor_tensor(stat[:, 6:7], stat[:, 6:7], stat[:, 7:8], ALU.mult))
                sv.inc(vec.tensor_scalar(stat[:, 6:7], stat[:, 6:7], EPS32, None, ALU.add))
                sv.wait(sca)
                ss.inc(sca.activation(stat[:, 8:9], stat[:, 6:7], AF.Ln))
                ss.inc(sca.activation(stat[:, 6:7], stat[:, 8:9], AF.Exp, scale=-0.5))
                ss.wait(vec)
                sv.inc(vec.scalar_tensor_tensor(scr[:], scr[:], stat[:, 6:7], wn[:], ALU.mult, ALU.mult))

        def emit_transpose_split(tt, dst_hi, dst_lo):
            sv.wait(ten)
            for ft in range(CT):
                pps = pa[:, 0:128] if ft % 2 == 0 else pb[:, 0:128]
                st.inc(ten.matmul(pps, scr[:, ft * 128:(ft + 1) * 128], c_id[:],
                                  start=True, stop=True, is_transpose=True))
                st.wait(vec)
                col = ft * 1024 + tt * 128
                sv.inc(vec.tensor_copy(dst_hi[:, col:col + 128], pps))
                sv.inc(vec.tensor_tensor(dst_lo[:, col:col + 128], pps, dst_hi[:, col:col + 128],
                                         ALU.subtract))
                sv.wait(ten)

        def emit_transpose_f8(tt):
            sv.wait(ten)
            for ft in range(CT):
                pps = pa[:, 0:128] if ft % 2 == 0 else pb[:, 0:128]
                st.inc(ten.matmul(pps, scr[:, ft * 128:(ft + 1) * 128], c_id[:],
                                  start=True, stop=True, is_transpose=True))
                st.wait(sca)
                col = ft * 1024 + tt * 128
                ss.inc(sca.activation(xn8[:, col:col + 128], pps, AF.Copy))
                ss.wait(ten)

        def tern_transpose(src, slab_idx, wsl, tsl, ttl, dstcols, tcol, war_wait=None):
            if war_wait is not None:
                sync.wait_ge(semV, war_wait)
            dW.inc(sync.dma_start(wsl, src[slab_idx * 128:(slab_idx + 1) * 128, :]), 16)
            dW.wait(vec)
            sv.inc(vec.tensor_scalar(scr[:], wsl, scal[:, tcol:tcol + 1], None, ALU.is_gt))
            sv.inc(vec.tensor_scalar(tsl[:], wsl, scal[:, tcol + 1:tcol + 2], None, ALU.is_lt))
            war = sv.n
            sv.inc(vec.tensor_tensor(tsl[:], scr[:], tsl[:], ALU.subtract))
            sv.wait(ten)
            for ft in range(CT):
                ppsb = pa[:, 0:512].bitcast(BF16)[:, 0:128] if ft % 2 == 0 else pb[:, 0:512].bitcast(BF16)[:, 0:128]
                st.inc(ten.matmul(ppsb, tsl[:, ft * 128:(ft + 1) * 128], c_idb[:],
                                  start=True, stop=True, is_transpose=True))
                st.wait(vec)
                sv.inc(vec.tensor_copy(ttl[:, dstcols + ft * 128:dstcols + (ft + 1) * 128], ppsb))
                sv.wait(ten)
            return war

        def mm2pass(ttl, e_cols, mv_hi, mv_lo, pdst):
            for half in range(2):
                first = True
                for ft in range(CT):
                    w_ap = ttl[:, e_cols + ft * 128:e_cols + (ft + 1) * 128]
                    mc = slice(ft * 1024 + half * 512, ft * 1024 + half * 512 + 512)
                    pd = pdst[:, half * 512:half * 512 + 512]
                    st.inc(ten.matmul(pd, w_ap, mv_hi[:, mc], start=first, stop=False))
                    st.inc(ten.matmul(pd, w_ap, mv_lo[:, mc], start=False, stop=(ft == CT - 1)))
                    first = False

        bcast_row(nrm[0:1, :], 0)
        with (
            sbt("hp_hi", [128, CT * 1024], BF16) as hp_hi,
            sbt("hp_lo", [128, CT * 1024], BF16) as hp_lo,
        ):
            with sbt("xld", [128, C], F32) as xld:
                for tt in range(NT):
                    dX.inc(gps.dma_start(xld[:], x_prev[tt * 128:(tt + 1) * 128, :]), 16)
                    dX.wait(vec)
                    emit_ln(x_prev, tt, xld[:], True)
                    emit_transpose_split(tt, hp_hi, hp_lo)
                    ss.wait(gps)
            barrier()
            with (
                sbt("wslab", [128, C], F32) as wslab,
                sbt("tslab", [128, C], BF16) as tslab,
                sbt("ttil", [128, 2 * 4096], BF16) as ttil,
                sbt("e_hi", [128, 2 * 1024], BF16) as e_hi,
                sbt("e_lo", [128, 2 * 1024], BF16) as e_lo,
                sbt("krt", [128, 1024], F32) as krt,
                sbt("kt_hi", [128, 1024], BF16) as kt_hi,
                sbt("kt_lo", [128, 1024], BF16) as kt_lo,
                sbt("lamr_t", [128, 128], F32) as lamr_t,
                sbt("vstash", [128, 512], BF16) as vstash,
            ):
                war = None
                head_mm_done = []
                for h in range(H):
                    tb = (h % 2) * 4096
                    if h >= 2:
                        sync.wait_ge(semT, head_mm_done[h - 2])
                    war = tern_transpose(qkv_w, 16 + h, wslab[:], tslab[:], ttil[:], tb + 0, 0, war)
                    war = tern_transpose(qkv_w, 32 + h, wslab[:], tslab[:], ttil[:], tb + 2048, 0, war)
                    sv.wait(sca)
                    ss.inc(sca.activation(lamr_t[:], c_ttrev[:], AF.Exp, scale=negs[:, h:h + 1]))
                    ss.wait(ten)
                    mm2pass(ttil, tb + 0, hp_hi, hp_lo, pk)
                    mm2pass(ttil, tb + 2048, hp_hi, hp_lo, pv)
                    head_mm_done.append(st.n)
                    st.wait(sca)
                    st.wait(vec)
                    ss.inc(sca.activation(e_hi[:, 0:1024], pk[:], AF.Copy, scale=scal[:, 2:3]))
                    ss.inc(sca.activation(e_hi[:, 1024:2048], pv[:], AF.Copy, scale=scal[:, 2:3]))
                    ss.wait(vec)
                    sv.inc(vec.scalar_tensor_tensor(e_lo[:, 0:1024], pk[:], scal[:, 2:3],
                                                    e_hi[:, 0:1024], ALU.mult, ALU.subtract))
                    sv.inc(vec.scalar_tensor_tensor(e_lo[:, 1024:2048], pv[:], scal[:, 2:3],
                                                    e_hi[:, 1024:2048], ALU.mult, ALU.subtract))
                    sv.inc(vec.tensor_tensor(krt[:], e_hi[:, 0:1024], e_lo[:, 0:1024], ALU.add))
                    for j in range(8):
                        jc = slice(j * 128, (j + 1) * 128)
                        sv.inc(vec.tensor_tensor(krt[:, jc], krt[:, jc], lamr_t[:], ALU.mult))
                    sv.inc(vec.tensor_copy(kt_hi[:], krt[:]))
                    sv.inc(vec.tensor_tensor(kt_lo[:], krt[:], kt_hi[:], ALU.subtract))
                    hc = slice(h * 128, (h + 1) * 128)
                    for j in range(8):
                        jc = slice(j * 128, (j + 1) * 128)
                        sv.wait(ten)
                        pt = pb[:, 0:512].bitcast(BF16)
                        st.inc(ten.matmul(pt[:, 0:128], kt_hi[:, jc], c_idb[:], start=True, stop=False, is_transpose=True))
                        st.inc(ten.matmul(pt[:, 128:256], kt_lo[:, jc], c_idb[:], start=False, stop=False, is_transpose=True))
                        st.inc(ten.matmul(pt[:, 256:384], e_hi[:, 1024 + j * 128:1024 + (j + 1) * 128], c_idb[:], start=False, stop=False, is_transpose=True))
                        st.inc(ten.matmul(pt[:, 384:512], e_lo[:, 1024 + j * 128:1024 + (j + 1) * 128], c_idb[:], start=False, stop=True, is_transpose=True))
                        st.wait(vec)
                        sv.inc(vec.tensor_copy(vstash[:], pt[:, 0:512]))
                        sv.wait(ten)
                        st.inc(ten.matmul(pa[:, 0:128], vstash[:, 0:128], vstash[:, 256:384], start=True, stop=False))
                        st.inc(ten.matmul(pa[:, 0:128], vstash[:, 0:128], vstash[:, 384:512], start=False, stop=False))
                        st.inc(ten.matmul(pa[:, 0:128], vstash[:, 128:256], vstash[:, 256:384], start=False, stop=True))
                        st.wait(vec)
                        sv.inc(vec.tensor_scalar(Sst[:, hc], Sst[:, hc], lamT[:, h:h + 1], None, ALU.mult))
                        sv.inc(vec.tensor_tensor(Sst[:, hc], Sst[:, hc], pa[:, 0:128], ALU.add))
            barrier()

        with sbt("xld", [128, C], F32) as xld:
            for tt in range(NT):
                dX.inc(gps.dma_start(xld[:], x_own[tt * 128:(tt + 1) * 128, :]), 16)
                dX.wait(vec)
                dX.wait(gps)
                dX.inc(gps.dma_start(out[tt * 128:(tt + 1) * 128, :], xld[:]), 16)
                dX.wait(sca)
                emit_ln(x_own, tt, xld[:], False)
                emit_transpose_f8(tt)
                ss.wait(gps)
        if debug:
            barrier()
            sv.inc(vec.tensor_copy(scr[:, 0:1024], xn8[:, 0:1024]))
            sv.wait(sync)
            dW.inc(sync.dma_start(dbg["d_xn8"][:], scr[:, 0:1024]), 16)
            dW.wait(vec)
        barrier()

        with sbt("gbuf", [128, CT * 1024], BF16) as gbuf:
            with (
                sbt("act8", [128, 64 * 1024], F8) as act8,
                sbt("w8", [128, 2 * CT * 1024], F8) as w8,
                sbt("ffg", [128, 8 * 1024], BF16) as ffg,
                sbt("fftmp", [128, 1024], BF16) as fftmp,
            ):
                w8done = []
                for hg in range(8):
                    for m in range(2):
                        slot = (hg * 2 + m) % 2
                        wcols = slice(slot * CT * 1024, (slot + 1) * CT * 1024)
                        if len(w8done) >= 2:
                            gps.wait_ge(semT, w8done[-2])
                        for ft in range(CT):
                            dX.inc(gps.dma_start(
                                w8[:, slot * CT * 1024 + ft * 1024: slot * CT * 1024 + (ft + 1) * 1024],
                                wg[m * C + ft * 128:m * C + (ft + 1) * 128, hg * 1024:(hg + 1) * 1024]), 16)
                        dX.wait(ten)
                        for ofl in range(8):
                            pd = pq if m == 0 else pk
                            for half in range(2):
                                first = True
                                for ft in range(CT):
                                    w_ap = w8[:, slot * CT * 1024 + ft * 1024 + ofl * 128:
                                              slot * CT * 1024 + ft * 1024 + (ofl + 1) * 128]
                                    mc = slice(ft * 1024 + half * 512, ft * 1024 + half * 512 + 512)
                                    st.inc(ten.matmul(pd[:, half * 512:half * 512 + 512], w_ap,
                                                      xn8[:, mc], start=first, stop=(ft == CT - 1)))
                                    first = False
                            if m == 0:
                                st.wait(sca)
                                ss.inc(sca.activation(fftmp[:], pq[:], AF.Copy))
                                ss.inc(sca.activation(ffg[:, ofl * 1024:(ofl + 1) * 1024],
                                                      fftmp[:], AF.Gelu_apprx_tanh))
                                ss.wait(ten)
                            else:
                                st.wait(sca)
                                ss.inc(sca.activation(fftmp[:], pk[:], AF.Copy))
                                ss.wait(vec)
                                sv.inc(vec.tensor_tensor(fftmp[:], ffg[:, ofl * 1024:(ofl + 1) * 1024],
                                                         fftmp[:], ALU.mult))
                                sv.wait(sca)
                                ss.inc(sca.activation(act8[:, (hg * 8 + ofl) * 1024:(hg * 8 + ofl + 1) * 1024],
                                                      fftmp[:], AF.Copy))
                                ss.wait(ten)
                        w8done.append(st.n)
                barrier()
                lindone = []
                for og in range(4):
                    for hk in range(64):
                        slot = hk % 2
                        if len(lindone) >= 2:
                            gps.wait_ge(semT, lindone[-2])
                        dX.inc(gps.dma_start(
                            w8[:, slot * 512: slot * 512 + 512],
                            wl[hk * 128:(hk + 1) * 128, og * 512:(og + 1) * 512]), 16)
                        dX.wait(ten)
                        for ofl in range(4):
                            w_ap = w8[:, slot * 512 + ofl * 128: slot * 512 + (ofl + 1) * 128]
                            if ofl < 3:
                                pds = (pq, pk, pv)[ofl]
                                for half in range(2):
                                    mc = slice(hk * 1024 + half * 512, hk * 1024 + half * 512 + 512)
                                    st.inc(ten.matmul(pds[:, half * 512:half * 512 + 512], w_ap,
                                                      act8[:, mc], start=(hk == 0), stop=(hk == 63)))
                            else:
                                st.inc(ten.matmul(pa[:, 0:512], w_ap, act8[:, hk * 1024:hk * 1024 + 512],
                                                  start=(hk == 0), stop=(hk == 63)))
                                st.inc(ten.matmul(pb[:, 0:512], w_ap, act8[:, hk * 1024 + 512:hk * 1024 + 1024],
                                                  start=(hk == 0), stop=(hk == 63)))
                        lindone.append(st.n)
                    st.wait(sca)
                    for ofl in range(4):
                        ot = og * 4 + ofl
                        if ofl < 3:
                            pds = (pq, pk, pv)[ofl]
                            ss.inc(sca.activation(fftmp[:], pds[:], AF.Copy))
                        else:
                            ss.inc(sca.activation(fftmp[:, 0:512], pa[:, 0:512], AF.Copy))
                            ss.inc(sca.activation(fftmp[:, 512:1024], pb[:, 0:512], AF.Copy))
                        ss.inc(sca.activation(gbuf[:, ot * 1024:(ot + 1) * 1024], fftmp[:], AF.Silu))
                    ss.wait(ten)
                if debug:
                    barrier()
                    sv.inc(vec.tensor_copy(scr[:, 0:1024], gbuf[:, 0:1024]))
                    sv.wait(sync)
                    dW.inc(sync.dma_start(dbg["d_g"][:], scr[:, 0:1024]), 16)
                    dW.wait(vec)
            barrier()

            with (
                sbt("h_hi", [128, CT * 1024], BF16) as h_hi,
                sbt("h_lo", [128, CT * 1024], BF16) as h_lo,
            ):
                with sbt("xld", [128, C], F32) as xld:
                    for tt in range(NT):
                        dX.inc(gps.dma_start(xld[:], x_own[tt * 128:(tt + 1) * 128, :]), 16)
                        dX.wait(vec)
                        emit_ln(x_own, tt, xld[:], True)
                        emit_transpose_split(tt, h_hi, h_lo)
                        ss.wait(gps)
                if debug:
                    barrier()
                    sv.inc(vec.tensor_tensor(scr[:, 0:1024], h_hi[:, 0:1024], h_lo[:, 0:1024], ALU.add))
                    sv.wait(sync)
                    dW.inc(sync.dma_start(dbg["d_h"][:], scr[:, 0:1024]), 16)
                    dW.wait(vec)
                barrier()
                esA = contextlib.ExitStack()
                with esA:
                    wslab = esA.enter_context(sbt("wslab", [128, C], F32))
                    tslab = esA.enter_context(sbt("tslab", [128, C], BF16))
                    ttil = esA.enter_context(sbt("ttil", [128, 2 * 6144], BF16))
                    qs_hi = esA.enter_context(sbt("qs_hi", [128, 1024], BF16))
                    qs_lo = esA.enter_context(sbt("qs_lo", [128, 1024], BF16))
                    ks_hi = esA.enter_context(sbt("ks_hi", [128, 1024], BF16))
                    ks_lo = esA.enter_context(sbt("ks_lo", [128, 1024], BF16))
                    vs_hi = esA.enter_context(sbt("vs_hi", [128, 1024], BF16))
                    vs_lo = esA.enter_context(sbt("vs_lo", [128, 1024], BF16))
                    qt_hi = esA.enter_context(sbt("qt_hi", [128, 1024], BF16))
                    qt_lo = esA.enter_context(sbt("qt_lo", [128, 1024], BF16))
                    kt_hi = esA.enter_context(sbt("kt_hi", [128, 1024], BF16))
                    kt_lo = esA.enter_context(sbt("kt_lo", [128, 1024], BF16))
                    lamt_t = esA.enter_context(sbt("lamt_t", [128, 128], F32))
                    lamr_t = esA.enter_context(sbt("lamr_t", [128, 128], F32))
                    maskt = esA.enter_context(sbt("maskt", [128, 128], F32))
                    sco = esA.enter_context(sbt("sco", [128, 128], F32))
                    sco_hi = esA.enter_context(sbt("sco_hi", [128, 128], BF16))
                    sco_lo = esA.enter_context(sbt("sco_lo", [128, 128], BF16))
                    s_hi = esA.enter_context(sbt("s_hi", [128, 128], BF16))
                    s_lo = esA.enter_context(sbt("s_lo", [128, 128], BF16))
                    vstash = esA.enter_context(sbt("vstash", [128, 512], BF16))
                    osm = esA.enter_context(sbt("osm", [128, 256], F32))
                    war = None
                    head_mm_done = []
                    osm_store = []
                    for h in range(H):
                        tb = (h % 2) * 6144
                        if h >= 2:
                            sync.wait_ge(semT, head_mm_done[h - 2])
                        war = tern_transpose(qkv_w, h, wslab[:], tslab[:], ttil[:], tb + 0, 0, war)
                        war = tern_transpose(qkv_w, 16 + h, wslab[:], tslab[:], ttil[:], tb + 2048, 0, war)
                        war = tern_transpose(qkv_w, 32 + h, wslab[:], tslab[:], ttil[:], tb + 4096, 0, war)
                        sv.wait(sca)
                        ss.inc(sca.activation(lamt_t[:], c_ttt[:], AF.Exp, scale=negs[:, h:h + 1]))
                        ss.inc(sca.activation(lamr_t[:], c_ttrev[:], AF.Exp, scale=negs[:, h:h + 1]))
                        ss.inc(sca.activation(maskt[:], c_difft[:], AF.Exp, scale=negs[:, h:h + 1]))
                        ss.wait(gps)
                        sg.inc(gps.affine_select(maskt[:], maskt[:], [[1, 128]], ALU.is_ge, 0.0,
                                                 base=0, channel_multiplier=-1))
                        sg.wait(vec)
                        sg.wait(ten)
                        if debug and h == 0:
                            sg.wait(sync)
                            ss.wait(sync)
                            dW.inc(sync.dma_start(dbg["d_mask"][:], maskt[:]), 16)
                            dW.inc(sync.dma_start(dbg["d_lamt"][:], lamt_t[:]), 16)
                            dW.inc(sync.dma_start(dbg["d_negs"][:], negs[:, 0:16]), 16)
                            dW.wait(sca)
                            dW.wait(gps)
                        ss.wait(ten)
                        mm2pass(ttil, tb + 0, h_hi, h_lo, pq)
                        mm2pass(ttil, tb + 2048, h_hi, h_lo, pk)
                        mm2pass(ttil, tb + 4096, h_hi, h_lo, pv)
                        head_mm_done.append(st.n)
                        st.wait(sca)
                        st.wait(vec)
                        ss.inc(sca.activation(qs_hi[:], pq[:], AF.Copy, scale=scal[:, 2:3]))
                        ss.inc(sca.activation(ks_hi[:], pk[:], AF.Copy, scale=scal[:, 2:3]))
                        ss.inc(sca.activation(vs_hi[:], pv[:], AF.Copy, scale=scal[:, 2:3]))
                        ss.wait(vec)
                        sv.inc(vec.scalar_tensor_tensor(qs_lo[:], pq[:], scal[:, 2:3], qs_hi[:], ALU.mult, ALU.subtract))
                        sv.inc(vec.scalar_tensor_tensor(ks_lo[:], pk[:], scal[:, 2:3], ks_hi[:], ALU.mult, ALU.subtract))
                        sv.inc(vec.scalar_tensor_tensor(vs_lo[:], pv[:], scal[:, 2:3], vs_hi[:], ALU.mult, ALU.subtract))
                        if debug and h == 0:
                            sv.wait(sync)
                            sv.inc(vec.tensor_tensor(scr[:, 0:1024], qs_hi[:], qs_lo[:], ALU.add))
                            sv.wait(sync)
                            dW.inc(sync.dma_start(dbg["d_q"][:], scr[:, 0:1024]), 16)
                            dW.wait(vec)
                            sv.inc(vec.tensor_tensor(scr[:, 0:1024], ks_hi[:], ks_lo[:], ALU.add))
                            sv.wait(sync)
                            dW.inc(sync.dma_start(dbg["d_k"][:], scr[:, 0:1024]), 16)
                            dW.wait(vec)
                            sv.inc(vec.tensor_tensor(scr[:, 0:1024], vs_hi[:], vs_lo[:], ALU.add))
                            sv.wait(sync)
                            dW.inc(sync.dma_start(dbg["d_v"][:], scr[:, 0:1024]), 16)
                            dW.wait(vec)
                        sv.inc(vec.tensor_tensor(scr[:, 0:1024], qs_hi[:], qs_lo[:], ALU.add))
                        for j in range(8):
                            jc = slice(j * 128, (j + 1) * 128)
                            sv.inc(vec.tensor_tensor(scr[:, jc], scr[:, jc], lamt_t[:], ALU.mult))
                        sv.inc(vec.tensor_copy(qt_hi[:], scr[:, 0:1024]))
                        sv.inc(vec.tensor_tensor(qt_lo[:], scr[:, 0:1024], qt_hi[:], ALU.subtract))
                        sv.inc(vec.tensor_tensor(scr[:, 0:1024], ks_hi[:], ks_lo[:], ALU.add))
                        for j in range(8):
                            jc = slice(j * 128, (j + 1) * 128)
                            sv.inc(vec.tensor_tensor(scr[:, jc], scr[:, jc], lamr_t[:], ALU.mult))
                        sv.inc(vec.tensor_copy(kt_hi[:], scr[:, 0:1024]))
                        sv.inc(vec.tensor_tensor(kt_lo[:], scr[:, 0:1024], kt_hi[:], ALU.subtract))
                        hc = slice(h * 128, (h + 1) * 128)
                        for j in range(8):
                            jc = slice(j * 128, (j + 1) * 128)
                            sv.wait(ten)
                            st.inc(ten.matmul(pa[:, 0:128], ks_hi[:, jc], qs_hi[:, jc], start=True, stop=False))
                            st.inc(ten.matmul(pa[:, 0:128], ks_hi[:, jc], qs_lo[:, jc], start=False, stop=False))
                            st.inc(ten.matmul(pa[:, 0:128], ks_lo[:, jc], qs_hi[:, jc], start=False, stop=True))
                            st.wait(vec)
                            sv.inc(vec.tensor_tensor(sco[:], pa[:, 0:128], maskt[:], ALU.mult))
                            sv.inc(vec.tensor_copy(sco_hi[:], sco[:]))
                            sv.inc(vec.tensor_tensor(sco_lo[:], sco[:], sco_hi[:], ALU.subtract))
                            sv.inc(vec.tensor_copy(s_hi[:], Sst[:, hc]))
                            sv.inc(vec.tensor_tensor(s_lo[:], Sst[:, hc], s_hi[:], ALU.subtract))
                            sv.wait(ten)
                            pt = pb[:, 0:512].bitcast(BF16)
                            st.inc(ten.matmul(pt[:, 0:128], vs_hi[:, jc], c_idb[:], start=True, stop=False, is_transpose=True))
                            st.inc(ten.matmul(pt[:, 128:256], vs_lo[:, jc], c_idb[:], start=False, stop=False, is_transpose=True))
                            st.inc(ten.matmul(pt[:, 256:384], kt_hi[:, jc], c_idb[:], start=False, stop=False, is_transpose=True))
                            st.inc(ten.matmul(pt[:, 384:512], kt_lo[:, jc], c_idb[:], start=False, stop=True, is_transpose=True))
                            st.wait(vec)
                            sv.inc(vec.tensor_copy(vstash[:], pt[:, 0:512]))
                            sv.wait(ten)
                            ss.wait(ten)
                            st.inc(ten.matmul(pq[:, 0:128], sco_hi[:], vstash[:, 0:128], start=True, stop=False))
                            st.inc(ten.matmul(pq[:, 0:128], sco_hi[:], vstash[:, 128:256], start=False, stop=False))
                            st.inc(ten.matmul(pq[:, 0:128], sco_lo[:], vstash[:, 0:128], start=False, stop=False))
                            st.inc(ten.matmul(pq[:, 0:128], qt_hi[:, jc], s_hi[:], start=False, stop=False))
                            st.inc(ten.matmul(pq[:, 0:128], qt_hi[:, jc], s_lo[:], start=False, stop=False))
                            st.inc(ten.matmul(pq[:, 0:128], qt_lo[:, jc], s_hi[:], start=False, stop=True))
                            st.inc(ten.matmul(pq[:, 512:640], vstash[:, 256:384], vstash[:, 0:128], start=True, stop=False))
                            st.inc(ten.matmul(pq[:, 512:640], vstash[:, 256:384], vstash[:, 128:256], start=False, stop=False))
                            st.inc(ten.matmul(pq[:, 512:640], vstash[:, 384:512], vstash[:, 0:128], start=False, stop=True))
                            st.wait(sca)
                            if len(osm_store) >= 2:
                                sca.wait_ge(semX, osm_store[-2])
                            ss.inc(sca.activation(osm[:, (j % 2) * 128:(j % 2) * 128 + 128], pq[:, 0:128],
                                                  AF.Copy, scale=SCALE_O))
                            ss.wait(gps)
                            dX.inc(gps.dma_start(o_scr[j * 128:(j + 1) * 128, h * 128:(h + 1) * 128],
                                                 osm[:, (j % 2) * 128:(j % 2) * 128 + 128]), 16)
                            osm_store.append(dX.n)
                            st.wait(vec)
                            ss.wait(vec)
                            sv.inc(vec.tensor_scalar(Sst[:, hc], Sst[:, hc], lamT[:, h:h + 1], None, ALU.mult))
                            sv.inc(vec.tensor_tensor(Sst[:, hc], Sst[:, hc], pq[:, 512:640], ALU.add))
                            if debug and h == 0 and j == 0:
                                sv.wait(sync)
                                ss.wait(sync)
                                dW.inc(sync.dma_start(dbg["d_sco"][:], sco[:]), 16)
                                dW.inc(sync.dma_start(dbg["d_S0"][:], Sst[:, 0:128]), 16)
                                dW.inc(sync.dma_start(dbg["d_oj0"][:], osm[:, 0:128]), 16)
                                dW.wait(vec)
                                dW.wait(sca)
                                sv.inc(vec.tensor_copy(scr[:, 0:512], vstash[:]))
                                sv.wait(sync)
                                dW.inc(sync.dma_start(dbg["d_vst"][:], scr[:, 0:512]), 16)
                                dW.wait(vec)
                        if debug and h == 0:
                            sv.wait(sync)
                            dW.inc(sync.dma_start(dbg["d_S"][:], Sst[:, 0:128]), 16)
                barrier()

            with (
                sbt("on_hi", [128, CT * 1024], BF16) as on_hi,
                sbt("on_lo", [128, CT * 1024], BF16) as on_lo,
                sbt("wn2", [128, C], F32) as wn2,
                sbt("xld", [128, C], F32) as xld,
            ):
                bcast_row(nrm[1:2, :], 0)
                dX.inc(gps.dma_start(wn2[:], nrm[2:3, :].broadcast_to([128, C])), 16)
                if debug:
                    barrier()
                    dW.inc(sync.dma_start(dbg["d_o"][:], o_scr[:]), 16)
                for tt in range(NT):
                    dX.inc(gps.dma_start(xld[:], o_scr[tt * 128:(tt + 1) * 128, :]), 16)
                    dX.wait(sca)
                    ss.inc(sca.activation(scr[:], xld[:], AF.Square, accum_out=stat[:, 0:1]))
                    ss.wait(vec)
                    sv.inc(vec.tensor_scalar(stat[:, 1:2], stat[:, 0:1], 1.0 / 2048.0, 1e-6, ALU.mult, ALU.add))
                    sv.wait(sca)
                    ss.inc(sca.activation(stat[:, 2:3], stat[:, 1:2], AF.Ln))
                    ss.inc(sca.activation(stat[:, 3:4], stat[:, 2:3], AF.Exp, scale=-0.5))
                    ss.wait(vec)
                    sv.inc(vec.scalar_tensor_tensor(scr[:], xld[:], stat[:, 3:4], wn[:], ALU.mult, ALU.mult))
                    sv.wait(sca)
                    ss.inc(sca.activation(xld[:], scr[:], AF.Square, accum_out=stat[:, 4:5]))
                    ss.wait(vec)
                    sv.inc(vec.tensor_scalar(stat[:, 5:6], stat[:, 4:5], 1.0 / 2048.0, EPS32, ALU.mult, ALU.add))
                    sv.wait(sca)
                    ss.inc(sca.activation(stat[:, 6:7], stat[:, 5:6], AF.Ln))
                    ss.inc(sca.activation(stat[:, 7:8], stat[:, 6:7], AF.Exp, scale=-0.5))
                    ss.wait(vec)
                    sv.inc(vec.scalar_tensor_tensor(scr[:], scr[:], stat[:, 7:8], wn2[:], ALU.mult, ALU.mult))
                    emit_transpose_split(tt, on_hi, on_lo)
                    sv.wait(gps)
                    ss.wait(gps)
                barrier()

                with (
                    sbt("wslab", [128, C], F32) as wslab,
                    sbt("tslab", [128, C], BF16) as tslab,
                    sbt("ttlp", [128, 2 * C], BF16) as ttlp,
                    sbt("soutt", [128, 1024], F32) as soutt,
                    sbt("pprod", [128, 1024], F32) as pprod,
                    sbt("outsm", [128, 2 * 128], F32) as outsm,
                ):
                    war = None
                    proj_done = []
                    outst = []
                    for ot in range(CT):
                        tb = (ot % 2) * C
                        if ot >= 2:
                            sync.wait_ge(semT, proj_done[ot - 2])
                        war = tern_transpose(proj_w, ot, wslab[:], tslab[:], ttlp[:], tb, 3, war)
                        sv.wait(ten)
                        ss.wait(ten)
                        mm2pass(ttlp, tb, on_hi, on_lo, pq)
                        proj_done.append(st.n)
                        st.wait(sca)
                        ss.inc(sca.activation(soutt[:], pq[:], AF.Copy, scale=scal[:, 5:6]))
                        ss.wait(vec)
                        sv.inc(vec.tensor_tensor(pprod[:], soutt[:], gbuf[:, ot * 1024:(ot + 1) * 1024], ALU.mult))
                        if debug and ot == 0:
                            sv.wait(sync)
                            dW.inc(sync.dma_start(dbg["d_sout"][:], soutt[:]), 16)
                        sv.wait(ten)
                        for tt in range(NT):
                            pslot = tt % 2
                            pd = pa[:, 0:128] if pslot == 0 else pb[:, 0:128]
                            st.inc(ten.matmul(pd, pprod[:, tt * 128:(tt + 1) * 128], c_id[:],
                                              start=True, stop=True, is_transpose=True))
                            st.wait(vec)
                            if len(outst) >= 2:
                                vec.wait_ge(semX, outst[-2])
                            sv.inc(vec.tensor_copy(outsm[:, pslot * 128:(pslot + 1) * 128], pd))
                            sv.wait(gps)
                            dX.inc(gps.dma_start(out[tt * 128:(tt + 1) * 128, ot * 128:(ot + 1) * 128],
                                                 outsm[:, pslot * 128:(pslot + 1) * 128],
                                                 accum_op=ALU.add), 16)
                            outst.append(dX.n)
                            sv.wait(ten)
                    barrier()

        def _replay(rec):
            def run(eng):
                for name, a, k, marker in rec.ops:
                    r = getattr(eng, name)(*a, **k)
                    if marker is not None:
                        for sem_, n_ in marker.incs:
                            r.then_inc(sem_, n_)
            return run

        block.sync(_replay(sync))
        block.gpsimd(_replay(gps))
        block.vector(_replay(vec))
        block.scalar(_replay(sca))
        block.tensor(_replay(ten))

    return nc


_NC_CACHE = {}


def _get_nc(debug=False):
    key = bool(debug)
    if key not in _NC_CACHE:
        _NC_CACHE[key] = build_nc(debug)
    return _NC_CACHE[key]


def kernel(L0_attn_x, L0_attn_qkv_w, L0_attn_qkv_norm_w, L0_attn_proj_w,
           L0_attn_proj_norm_w, L0_attn_norm_w, L0_attn_s, L0_ffn_w_gating,
           L0_ffn_w_linear, _debug=False):
    x = np.ascontiguousarray(np.asarray(L0_attn_x, dtype=np.float32))
    qkv_w = np.ascontiguousarray(np.asarray(L0_attn_qkv_w, dtype=np.float32))
    proj_w = np.ascontiguousarray(np.asarray(L0_attn_proj_w, dtype=np.float32))
    wgm = np.ascontiguousarray(np.asarray(L0_ffn_w_gating, dtype=np.float32)).reshape(2 * C, HID)
    wlm = np.ascontiguousarray(np.asarray(L0_ffn_w_linear, dtype=np.float32))
    nrm = np.ascontiguousarray(np.stack([np.asarray(L0_attn_qkv_norm_w, dtype=np.float32),
                                         np.asarray(L0_attn_norm_w, dtype=np.float32),
                                         np.asarray(L0_attn_proj_norm_w, dtype=np.float32)], axis=0))
    svec = np.ascontiguousarray(np.asarray(L0_attn_s, dtype=np.float32).reshape(1, H))

    ident = np.eye(128, dtype=np.float32)
    identb = np.eye(128, dtype=ml_dtypes.bfloat16)
    onescol = np.ones((128, 1), dtype=np.float32)
    onesrow = np.ones((1, 128), dtype=np.float32)
    ar = np.arange(128, dtype=np.float32)
    ttt = np.ascontiguousarray(np.broadcast_to(ar, (128, 128)))
    ttrev = np.ascontiguousarray(np.broadcast_to(128.0 - ar, (128, 128)))
    difft = np.ascontiguousarray(ar[None, :] - ar[:, None])
    zeros_prev = np.zeros((1024, C), dtype=np.float32)

    nc = _get_nc(_debug)
    in_maps = []
    for c in range(8):
        b, half = c // 2, c % 2
        xo = np.ascontiguousarray(x[b, half * 1024:(half + 1) * 1024])
        xp = np.ascontiguousarray(x[b, 0:1024]) if half == 1 else zeros_prev
        in_maps.append({
            "x_own": xo, "x_prev": xp, "qkv_w": qkv_w, "proj_w": proj_w,
            "wg": wgm, "wl": wlm, "nrm": nrm, "svec": svec,
            "ident": ident, "identb": identb, "onescol": onescol, "onesrow": onesrow,
            "ttt": ttt, "ttrev": ttrev, "difft": difft,
        })
    res = run_bass_kernel_spmd(nc, in_maps, core_ids=list(range(8)))
    outp = np.empty((4, C, C), dtype=np.float32)
    for c in range(8):
        b, half = c // 2, c % 2
        outp[b, half * 1024:(half + 1) * 1024] = np.asarray(res.results[c]["out"])
    if _debug:
        return outp, res
    return outp
